# version 16
# baseline (speedup 1.0000x reference)
import numpy as np
import ml_dtypes
import concourse.bass as bass
import concourse.mybir as mybir
from concourse.bass_utils import run_bass_kernel_spmd

F32 = mybir.dt.float32
BF16 = mybir.dt.bfloat16
I32 = mybir.dt.int32
AF = mybir.ActivationFunctionType
ALU = mybir.AluOpType

NCORES = 8
T = 1024
TH = 512
DC = 32
NH = 8
EPS_ATTN = 1e-6
EPS_LN = 1e-5

_CACHE = {}


class Prog:

    def __init__(self):
        self.ev = {}
        self.dry = True

    def rec(self, name, sem, k=1):
        sem.n += k
        self.ev[name] = sem.n

    def chk(self, name, sem, k=1):
        sem.n += k
        assert self.ev[name] == sem.n, (name, self.ev[name], sem.n)


class Sem:
    def __init__(self, nc, name):
        self.h = nc.alloc_semaphore(name)
        self.name = name
        self.n = 0


def _mk_waiter(eng, dry, P):
    last = {}

    def wait(sem, val):
        if dry:
            return
        if isinstance(val, str):
            val = P.ev[val]
        if val <= 0:
            return
        if last.get(sem.name, 0) >= val:
            return
        last[sem.name] = val
        eng.wait_ge(sem.h, val)

    return wait


def build_nc(debug_dump=False):
    nc = bass.Bass("TRN2", target_bir_lowering=False,
                   detect_race_conditions=False)

    d_x = nc.dram_tensor("xh", [DC, 128, T], F32, kind="ExternalInput").ap()
    d_wq = nc.dram_tensor("wq", [NH, 128, 4096], F32, kind="ExternalInput").ap()
    d_bq = nc.dram_tensor("bqi", [128, NH], F32, kind="ExternalInput").ap()
    d_m = nc.dram_tensor("mi", [NH, 128, 128], F32, kind="ExternalInput").ap()
    d_zm = nc.dram_tensor("zmk", [128, 64], F32, kind="ExternalInput").ap()
    d_wo = nc.dram_tensor("wo", [DC, 128, 4096], F32, kind="ExternalInput").ap()
    d_qw = nc.dram_tensor("qwi", [DC, 128, 4096], I32, kind="ExternalInput").ap()
    d_sc = nc.dram_tensor("sci", [128, 1024], F32, kind="ExternalInput").ap()
    d_zr = nc.dram_tensor("zri", [128, 1024], F32, kind="ExternalInput").ap()
    d_fb = nc.dram_tensor("fbr", [1, 4096], F32, kind="ExternalInput").ap()
    d_ep = nc.dram_tensor("epat", [8, 1024], BF16, kind="ExternalInput").ap()
    d_on = nc.dram_tensor("onesr", [1, 128], F32, kind="ExternalInput").ap()
    d_sl = nc.dram_tensor("slhs", [128, 16], BF16, kind="ExternalInput").ap()
    d_s4 = nc.dram_tensor("sel4", [4, 4], F32, kind="ExternalInput").ap()
    d_id = nc.dram_tensor("idnt", [128, 128], BF16, kind="ExternalInput").ap()
    d_cv = nc.dram_tensor("cvals", [1, 4], F32, kind="ExternalInput").ap()
    d_out = nc.dram_tensor("outT", [DC, 128, T], F32, kind="ExternalOutput").ap()
    if debug_dump:
        d_dbgA = nc.dram_tensor("dbgA", [128, DC * T], BF16, kind="ExternalOutput").ap()
        d_dbgB = nc.dram_tensor("dbgB", [128, DC * T], BF16, kind="ExternalOutput").ap()
        d_dbgS = nc.dram_tensor("dbgS", [128, NH * T], BF16, kind="ExternalOutput").ap()
        d_dbgR = nc.dram_tensor("dbgR", [128, NH * T], BF16, kind="ExternalOutput").ap()
        d_dbgrows = nc.dram_tensor("dbgrows", [1, 2048], F32, kind="ExternalOutput").ap()
        d_dbgw1 = nc.dram_tensor("dbgw1", [128, 32], F32, kind="ExternalOutput").ap()
        d_dbgmb = nc.dram_tensor("dbgmb", [128, 1024], BF16, kind="ExternalOutput").ap()
        d_dbgFf = nc.dram_tensor("dbgFf", [128, 2048], F32, kind="ExternalOutput").ap()

    A = nc.alloc_sbuf_tensor("A", [128, DC * T], BF16)
    B = nc.alloc_sbuf_tensor("B", [128, DC * T], BF16)
    S = nc.alloc_sbuf_tensor("S", [128, NH * T], BF16)
    R = nc.alloc_sbuf_tensor("R", [128, NH * T], BF16)
    W = nc.alloc_sbuf_tensor("W", [128, 8192], BF16)
    Ff = nc.alloc_sbuf_tensor("Ff", [128, 2048], F32)
    TT2 = nc.alloc_sbuf_tensor("TT2", [128, 1024], F32)
    usq = nc.alloc_sbuf_tensor("usq", [128, 2 * TH], BF16)
    mbf = nc.alloc_sbuf_tensor("mbf", [128, NH * 128], BF16)
    rows = nc.alloc_sbuf_tensor("rows", [1, 2048], F32)
    rowsb = nc.alloc_sbuf_tensor("rowsb", [1, 1024], BF16)
    bqs = nc.alloc_sbuf_tensor("bqs", [128, NH], F32)
    nbq = nc.alloc_sbuf_tensor("nbq", [128, NH], F32)
    zmks = nc.alloc_sbuf_tensor("zmks", [128, 64], BF16)
    epats = nc.alloc_sbuf_tensor("epats", [8, 1024], BF16)
    onesr = nc.alloc_sbuf_tensor("onesrs", [1, 128], F32)
    slhss = nc.alloc_sbuf_tensor("slhss", [128, 16], BF16)
    sel4s = nc.alloc_sbuf_tensor("sel4s", [4, 4], F32)
    idnts = nc.alloc_sbuf_tensor("idnts", [128, 128], BF16)
    cvs = nc.alloc_sbuf_tensor("cvs", [1, 4], F32)
    nc.const_aps.register(F32, EPS_LN, cvs[0:1, 0:1])
    nc.const_aps.register(F32, 0.5, cvs[0:1, 1:2])
    nc.const_aps.register(F32, -0.5, cvs[0:1, 2:3])

    p_q = nc.alloc_psum_tensor("p_q", [128, 1024], F32)
    p_dn = nc.alloc_psum_tensor("p_dn", [128, 1024], F32)
    p_b1 = nc.alloc_psum_tensor("p_b1", [128, 512], F32)
    p_t0 = nc.alloc_psum_tensor("p_t0", [128, 1024], BF16)
    p_t1 = nc.alloc_psum_tensor("p_t1", [128, 1024], BF16)
    p_st = nc.alloc_psum_tensor("p_st", [128, 512], F32)
    rem = nc.sbuf_bytes_remaining
    assert rem >= 0, f"SBUF over budget by {-rem}"

    sG = Sem(nc, "sG")
    sT = Sem(nc, "sT")
    sV = Sem(nc, "sV")
    sS = Sem(nc, "sS")
    s_x = [Sem(nc, f"s_x{k}") for k in range(4)]
    s_wq = [Sem(nc, f"s_wq{k}") for k in range(2)]
    s_m = Sem(nc, "s_m")
    s_fb = Sem(nc, "s_fb")
    s_wo = [Sem(nc, f"s_wo{k}") for k in range(2)]
    s_qw = [Sem(nc, f"s_qw{k}") for k in range(2)]
    s_c = Sem(nc, "s_c")
    s_sz = Sem(nc, "s_sz")
    s_st = [Sem(nc, f"s_st{k}") for k in range(2)]

    P = Prog()

    def Ach(c, half=None):
        if half is None:
            return A[:, c * T:(c + 1) * T]
        return A[:, c * T + half * TH: c * T + (half + 1) * TH]

    def Bch(c, half=None):
        if half is None:
            return B[:, c * T:(c + 1) * T]
        return B[:, c * T + half * TH: c * T + (half + 1) * TH]

    def Ssig(h, half=None):
        if half is None:
            return S[:, h * T:(h + 1) * T]
        return S[:, h * T + half * TH: h * T + (half + 1) * TH]

    def Rretr(h, half=None):
        if half is None:
            return R[:, h * T:(h + 1) * T]
        return R[:, h * T + half * TH: h * T + (half + 1) * TH]

    def Wslot(i):
        return W[:, (i % 2) * 4096:(i % 2) * 4096 + 4096]

    def SwT(oc):
        return S[:, (oc % 2) * 4096:(oc % 2) * 4096 + 4096]

    r_addr = nc.lookup_mloc(R).addr
    ost = nc.alloc_sbuf_tensor_at("ost", [128, 2048], F32, offset=r_addr + 8192)
    out_stage = ost[:]
    s_addr = nc.lookup_mloc(S).addr
    p4r = nc.alloc_sbuf_tensor_at("p4r", [128, 2048], F32, offset=s_addr)


    def Oslot(oc, half=None):
        base = (oc % 2) * 1024
        if half is None:
            return out_stage[:, base:base + 1024]
        return out_stage[:, base + half * TH: base + (half + 1) * TH]

    fb_row = R[0:1, 0:4096]
    den = Ff[0:8, 0:1024]
    rden = Ff[0:8, 1024:2048]
    rden_bfT = nc.alloc_sbuf_tensor("rden_bfT", [8, 1024], BF16)
    rden_bf = rden_bfT[:]
    w1scr = nc.alloc_sbuf_tensor("w1scr", [128, 64], F32)
    w1n = nc.alloc_sbuf_tensor("w1n", [128, 32], F32)
    sc_img = Ff[:, 0:1024]
    zr_img = Ff[:, 1024:2048]
    t1 = TT2[:, 0:512]
    t2 = TT2[:, 512:1024]
    m_slot = rows[0:1, 0:1024]
    v_slot = rows[0:1, 1024:2048]

    ext_dst = TT2[0:4, 512:1024]

    def usqh(half):
        return usq[:, half * 512:(half + 1) * 512]

    def bcb(k2):
        return p_b1[:] if (k2 % 2 == 0) else p_st[:]

    def pool4(j):
        tsr = p_q if j < 2 else p_dn
        return tsr[:, (j % 2) * 512:(j % 2) * 512 + 512]

    def pq(bank, half=None):
        if half is None:
            return p_q[:, bank * 512:(bank + 1) * 512]
        return p_q[:, bank * 512:(bank + 1) * 512]

    def prog_gpsimd(g, dry):
        wt = _mk_waiter(g, dry, P)
        rec = (P.rec if dry else P.chk)

        def dma(out, in_, sem, name):
            if not dry:
                g.dma_start(out=out, in_=in_).then_inc(sem.h, 16)
            rec(name, sem, 16)

        dma(zmks[:], d_zm, s_m, "g_zm")
        dma(Wslot(0), d_wq[0], s_wq[0], "g_wq0")
        dma(A[:, 0:8192].rearrange("p (c t) -> p c t", c=8),
            d_x[0:8].rearrange("c p t -> p c t"), s_x[0], "g_xg0")
        dma(Wslot(1), d_wq[1], s_wq[1], "g_wq1")
        for h in range(NH):
            dma(mbf[:, h * 128:(h + 1) * 128], d_m[h], s_m, f"g_m{h}")
        for gi in range(1, 4):
            dma(A[:, gi * 8192:(gi + 1) * 8192].rearrange("p (c t) -> p c t", c=8),
                d_x[8 * gi:8 * gi + 8].rearrange("c p t -> p c t"),
                s_x[gi], f"g_xg{gi}")
        for h in range(2, NH):
            wt(sT, f"t_q{h-2}_1")
            dma(Wslot(h), d_wq[h], s_wq[h % 2], f"g_wq{h}")
        for oc in range(DC):
            i = NH + oc
            if oc == 0:
                wt(sT, "t_q6_1")
            elif oc == 1:
                wt(sT, "t_q7_1")
            else:
                wt(sV, f"v_fold{oc-2}")
            dma(Wslot(i)[:, 0:2048], d_wo[oc][:, 0:2048], s_wo[oc % 2],
                f"g_wo{oc}a")
            dma(Wslot(i)[:, 2048:4096], d_wo[oc][:, 2048:4096], s_wo[oc % 2],
                f"g_wo{oc}b")
        wt(sV, "v_fold30")
        dma(Wslot(NH + DC + 0), d_qw[0], s_qw[0], "g_qw0")
        wt(sV, "v_fold31")
        dma(Wslot(NH + DC + 1), d_qw[1], s_qw[1], "g_qw1")
        wt(sT, "t_att31_1")
        dma(fb_row, d_fb, s_fb, "g_fb")
        for oc in range(2, DC):
            i = NH + DC + oc
            wt(sT, f"t_trd{oc-2}")
            wt(sV, f"v_w1{oc-2}")
            dma(Wslot(i), d_qw[oc], s_qw[oc % 2], f"g_qw{oc}")


    def prog_sync(sy, dry):
        wt = _mk_waiter(sy, dry, P)
        rec = (P.rec if dry else P.chk)

        def dma(out, in_, sem, name):
            if not dry:
                sy.dma_start(out=out, in_=in_).then_inc(sem.h, 16)
            rec(name, sem, 16)

        dma(bqs[:], d_bq, s_c, "c_bq")
        dma(epats[:], d_ep, s_c, "c_ep")
        dma(onesr[:], d_on, s_c, "c_on")
        dma(slhss[:], d_sl, s_c, "c_sl")
        dma(sel4s[:], d_s4, s_c, "c_s4")
        dma(idnts[:], d_id, s_c, "c_id")
        dma(cvs[:], d_cv, s_c, "c_cv")
        wt(sV, "v_rden")
        dma(sc_img, d_sc, s_sz, "c_sc")
        wt(sT, "t_bc7_1")
        dma(zr_img, d_zr, s_sz, "c_zr")
        for oc in range(DC):
            wt(sV, f"v_out{oc}")
            dma(d_out[oc], Oslot(oc), s_st[oc % 2], f"st{oc}")
        if debug_dump:
            wt(sV, "v_out31")
            for nm, dd, ss in (("dA", A, d_dbgA), ("dB", B, d_dbgB),
                               ("dS", S, d_dbgS), ("dR", R, d_dbgR),
                               ("drw", rows, d_dbgrows), ("dFf", Ff, d_dbgFf),
                               ("dw1", w1n, d_dbgw1), ("dmb", mbf, d_dbgmb)):
                dma(ss, dd[:], s_c, nm)
        if not dry:
            sy.wait_ge(s_st[0].h, (DC // 2) * 16)
            sy.wait_ge(s_st[1].h, (DC // 2) * 16)
            if debug_dump:
                sy.wait_ge(s_c.h, P.ev["dFf"])

    def prog_tensor(t, dry):
        wt = _mk_waiter(t, dry, P)
        rec = (P.rec if dry else P.chk)

        def mm(out, lhsT, rhs, start, stop, **kw):
            if not dry:
                return t.matmul(out, lhsT, rhs, start=start, stop=stop, **kw)
            return None

        def den(h):
            wt(s_m, 16 * 9)
            wt(sV, f"v_sig{h}_1")
            for half in range(2):
                i = mm(p_dn[0:8, half * 512:(half + 1) * 512],
                       zmks[:, h * 8:(h + 1) * 8], Ssig(h, half),
                       start=(h == 0), stop=(h == NH - 1), skip_group_check=True)
            if h == NH - 1:
                if not dry:
                    i.then_inc(sT.h, 1)
                rec("t_den", sT)

        for h in range(NH):
            wt(s_wq[h % 2], 16 * (h // 2 + 1))
            for half in range(2):
                bank = half
                if h > 0:
                    wt(sS, f"s_relu{h-1}_{half}")
                for c in range(DC):
                    if h == 0 and half == 0:
                        wt(s_x[c // 8], 16)
                    i = mm(pq(bank), Wslot(h)[:, c * 128:(c + 1) * 128],
                           Ach(c, half), start=(c == 0), stop=(c == DC - 1))
                if not dry:
                    i.then_inc(sT.h, 1)
                rec(f"t_q{h}_{half}", sT)
            if h >= 2:
                den(h - 2)
        den(NH - 2)
        den(NH - 1)
        for k2 in range(17):
            if k2 < 16:
                h, half = divmod(k2, 2)
                wt(sV, "v_rden")
                if k2 >= 2:
                    ph, phalf = divmod(k2 - 2, 2)
                    wt(sV, f"v_qsel{ph}_{phalf}")
                i = mm(bcb(k2), epats[:, h * 128:(h + 1) * 128],
                       rden_bf[:, half * 512:(half + 1) * 512], True, True)
                if not dry:
                    i.then_inc(sT.h, 1)
                rec(f"t_bc{h}_{half}", sT)
            if k2 >= 1:
                h, half = divmod(k2 - 1, 2)
                wt(sV, f"v_qsel{h}_{half}")
                wt(s_m, 16 * 9)
                if k2 - 1 >= 2:
                    ph, phalf = divmod(k2 - 3, 2)
                    wt(sS, f"s_retr{ph}_{phalf}")
                i = mm(p_dn[:, ((k2 - 1) % 2) * 512:((k2 - 1) % 2) * 512 + 512],
                       mbf[:, h * 128:(h + 1) * 128], Ssig(h, half), True, True)
                if not dry:
                    i.then_inc(sT.h, 1)
                rec(f"t_num{h}_{half}", sT)
        def st1(oc, half):
            wt(sS, f"s_usq{oc}_{half}")
            first = (oc == 0 and half == 0)
            if first:
                wt(sV, "v_qsel7_1")
            mm(p_st[0:4, :], slhss[:, (2 * half) * 4:(2 * half) * 4 + 4],
               Ach(oc, half), start=first, stop=False, skip_group_check=True)
            i = mm(p_st[0:4, :], slhss[:, (2 * half + 1) * 4:(2 * half + 1) * 4 + 4],
                   usqh(half), start=False, stop=(oc == DC - 1 and half == 1),
                   skip_group_check=True)
            if not dry:
                i.then_inc(sT.h, 1)
            rec(f"t_st1_{oc}_{half}", sT)

        wt(sS, "s_retr7_1")
        for oc in range(DC):
            wt(sV, f"v_fold{oc}")
            for half in range(2):
                bank = (2 * oc + half) % 4
                if oc > 1:
                    wt(sV, f"v_res{oc-2}_{half}")
                for h in range(NH):
                    wt(sS, f"s_retr{h}_{half}")
                    i = mm(pool4(bank), Bch(oc)[:, h * 128:(h + 1) * 128],
                           Rretr(h, half), start=(h == 0), stop=(h == NH - 1))
                if not dry:
                    i.then_inc(sT.h, 1)
                rec(f"t_att{oc}_{half}", sT)
            if oc > 0:
                st1(oc - 1, 0)
                st1(oc - 1, 1)
        st1(DC - 1, 0)
        st1(DC - 1, 1)
        wt(sV, "v_st1evac")
        extb = (p_b1[0:1, :], p_st[0:1, :], p_q[0:1, 0:512], p_q[0:1, 512:1024])
        for j in range(4):
            i = mm(extb[j], sel4s[:, j:j + 1], ext_dst, True, True)
        if not dry:
            i.then_inc(sT.h, 1)
        rec(f"t_ext1", sT)
        wt(sV, "v_rows1")
        wt(sV, "v_res31_0")
        wt(sV, "v_res31_1")
        for half in range(2):
            i = mm(pq(half), onesr[:], m_slot[:, half * 512:(half + 1) * 512],
                   True, True)
        if not dry:
            i.then_inc(sT.h, 1)
        rec("t_m1b", sT)
        def trgrp(oc, grp):
            k = oc * 4 + grp
            pt = p_t0 if (k % 2 == 0) else p_t1
            if k >= 2:
                pk = k - 2
                if pk % 2 == 0:
                    wt(sV, f"v_tre{pk}")
                else:
                    wt(sS, f"s_tre{pk}")
            for j in range(8):
                g = grp * 8 + j
                if not dry:
                    i = t.transpose(pt[:, j * 128:(j + 1) * 128],
                                    Wslot(NH + DC + oc)[:, g * 128:(g + 1) * 128],
                                    idnts[:])
            if not dry:
                i.then_inc(sT.h, 1)
            rec(f"t_tr{oc}_{grp}", sT)
            if grp == 3:
                P.ev[f"t_trd{oc}"] = P.ev[f"t_tr{oc}_3"]

        def fhalf(oc, half):
            wt(sV, "v_m1row")
            wt(sV, f"v_tre{oc*4+2}")
            wt(sS, f"s_tre{oc*4+3}")
            wt(s_fb, 16)
            wt(sV, "v_s1dbf")
            bank = (2 * oc + half) % 4
            if oc > 1:
                wt(sV, f"v_V{oc-2}_{half}")
            for g in range(DC):
                mm(pool4(bank), SwT(oc)[:, g * 128:(g + 1) * 128],
                   Ach(g, half), start=(g == 0), stop=False)
            i = mm(pool4(bank), fb_row[:, oc * 128:(oc + 1) * 128],
                   rowsb[:, half * 512:(half + 1) * 512],
                   start=False, stop=True)
            if not dry:
                i.then_inc(sT.h, 1)
            rec(f"t_f{oc}_{half}", sT)

        def st2(oc, half):
            if oc == 0 and half == 0:
                wt(sV, "v_st1evac")
            wt(sS, f"s_vsq{oc}_{half}")
            first = (oc == 0 and half == 0)
            mm(p_st[0:4, :], slhss[:, (2 * half) * 4:(2 * half) * 4 + 4],
               Bch(oc, half), start=first, stop=False, skip_group_check=True)
            i = mm(p_st[0:4, :], slhss[:, (2 * half + 1) * 4:(2 * half + 1) * 4 + 4],
                   usqh(half), start=False, stop=(oc == DC - 1 and half == 1),
                   skip_group_check=True)
            if not dry:
                i.then_inc(sT.h, 1)
            rec(f"t_st2_{oc}_{half}", sT)

        for k in range(DC + 2):
            if k < DC:
                wt(sV, f"v_w1{k}")
                trgrp(k, 0)
                trgrp(k, 1)
            if 1 <= k <= DC:
                fhalf(k - 1, 0)
            if k < DC:
                trgrp(k, 2)
                trgrp(k, 3)
            if 1 <= k <= DC:
                fhalf(k - 1, 1)
            if 2 <= k:
                st2(k - 2, 0)
                st2(k - 2, 1)
        wt(sV, "v_st2evac")
        extb = (p_b1[0:1, :], p_st[0:1, :], p_q[0:1, 0:512], p_q[0:1, 512:1024])
        for j in range(4):
            i = mm(extb[j], sel4s[:, j:j + 1], ext_dst, True, True)
        if not dry:
            i.then_inc(sT.h, 1)
        rec(f"t_ext2", sT)
        wt(sV, "v_c2")
        wt(sV, "v_V31_0")
        wt(sV, "v_V31_1")
        for half in range(2):
            mm(pq(half), onesr[:], v_slot[:, half * 512:(half + 1) * 512],
               True, True)
            i = mm(p_dn[:, half * 512:(half + 1) * 512], onesr[:],
                   m_slot[:, half * 512:(half + 1) * 512], True, True)
        if not dry:
            i.then_inc(sT.h, 1)
        rec("t_p4b", sT)

    def prog_vector(v, dry):
        wt = _mk_waiter(v, dry, P)
        rec = (P.rec if dry else P.chk)

        wt(s_c, 16 * 7)
        if not dry:
            v.tensor_scalar_mul(nbq[:], bqs[:], -1.0)
        for h in range(NH):
            for half in range(2):
                bank = half
                wt(sT, f"t_q{h}_{half}")
                if not dry:
                    v.tensor_single_scalar(t1[:], pq(bank), nbq[:, h:h + 1],
                                           ALU.min).then_inc(sV.h, 1)
                rec(f"v_t1{h}_{half}", sV)
                wt(sS, f"s_relu{h}_{half}")
                if not dry:
                    v.tensor_add(Ssig(h, half), t1[:], t2[:]).then_inc(sV.h, 1)
                rec(f"v_sig{h}_{half}", sV)
        wt(sT, "t_den")
        if not dry:
            v.tensor_single_scalar(den[:], p_dn[0:8, :], EPS_ATTN, ALU.add)
            v.reciprocal(rden[:], den[:])
            v.tensor_copy(rden_bf, rden[:]).then_inc(sV.h, 1)
        rec("v_rden", sV)
        for h in range(NH):
            for half in range(2):
                k2 = 2 * h + half
                wt(sT, f"t_bc{h}_{half}")
                if not dry:
                    v.tensor_mul(Ssig(h, half), Ssig(h, half),
                                 bcb(k2)).then_inc(sV.h, 1)
                rec(f"v_qsel{h}_{half}", sV)
        def emit_fold(oc):
            wt(s_wo[oc % 2], 32 * (oc // 2 + 1))
            ws = Wslot(NH + oc)[:].rearrange("p (h g o) -> p h g o", h=8, g=4)
            bo = Bch(oc).rearrange("p (h o) -> p h o", h=8)
            if not dry:
                v.tensor_add(bo, ws[:, :, 0, :], ws[:, :, 1, :])
                v.tensor_add(bo, bo, ws[:, :, 2, :])
                v.tensor_add(bo, bo, ws[:, :, 3, :]).then_inc(sV.h, 1)
            rec(f"v_fold{oc}", sV)

        emit_fold(0)
        emit_fold(1)
        for oc in range(DC):
            for half in range(2):
                bank = (2 * oc + half) % 4
                wt(sT, f"t_att{oc}_{half}")
                if not dry:
                    v.tensor_add(Ach(oc, half), Ach(oc, half),
                                 pool4(bank)).then_inc(sV.h, 1)
                rec(f"v_res{oc}_{half}", sV)
            if oc + 2 < DC:
                emit_fold(oc + 2)
        wt(sT, "t_st1_31_1")
        if not dry:
            v.tensor_copy(ext_dst, p_st[0:4, :]).then_inc(sV.h, 1)
        rec("v_st1evac", sV)
        wt(sT, "t_ext1")
        extb = (p_b1[0:1, :], p_st[0:1, :], p_q[0:1, 0:512], p_q[0:1, 512:1024])
        if not dry:
            for j in range(4):
                dst = (m_slot[:, 0:512], v_slot[:, 0:512],
                       m_slot[:, 512:1024], v_slot[:, 512:1024])[j]
                i = v.tensor_copy(dst, extb[j])
            i.then_inc(sV.h, 1)
        rec(f"v_ext1", sV)
        if not dry:
            v.tensor_scalar_mul(m_slot, m_slot, 1.0 / 4096.0)
            v.tensor_scalar_mul(v_slot, v_slot, 1.0 / 4096.0)
            v.tensor_mul(TT2[0:1, 0:1024], m_slot, m_slot)
            v.tensor_sub(v_slot, v_slot, TT2[0:1, 0:1024])
            v.tensor_single_scalar(v_slot, v_slot, EPS_LN,
                                   ALU.add).then_inc(sV.h, 1)
        rec("v_rows1", sV)
        wt(sS, "s_s1d")
        if not dry:
            v.tensor_copy(rowsb[:], v_slot).then_inc(sV.h, 1)
        rec("v_s1dbf", sV)
        def emit_deq(oc):
            wt(s_qw[oc % 2], 16 * (oc // 2 + 1))
            wt(s_sz, 32)
            if oc >= 2:
                wt(sT, f"t_trd{oc-2}")
            ws = Wslot(NH + DC + oc)
            for g in range(32):
                col = oc * 32 + g
                if not dry:
                    ts = v.tensor_scalar(ws[:, g * 128:(g + 1) * 128],
                                         ws[:, g * 128:(g + 1) * 128],
                                         sc_img[:, col:col + 1],
                                         zr_img[:, col:col + 1],
                                         ALU.mult, ALU.add)
            if not dry:
                ts.then_inc(sV.h, 1)
            rec(f"v_deq{oc}", sV)

        def emit_w1(oc):
            wt(sV, f"v_deq{oc}")
            ws = Wslot(NH + DC + oc)
            if not dry:
                v.tensor_reduce(w1scr[:, oc % 2:oc % 2 + 1], ws[:],
                                mybir.AxisListType.X,
                                ALU.add).then_inc(sV.h, 1)
            rec(f"v_red{oc}", sV)
            wt(sV, f"v_red{oc}")
            if not dry:
                v.tensor_scalar(w1n[:, oc:oc + 1],
                                w1scr[:, oc % 2:oc % 2 + 1], 1.0, -1.0,
                                ALU.add, ALU.mult).then_inc(sV.h, 1)
            rec(f"v_w1{oc}", sV)

        emit_deq(0)
        emit_w1(0)
        emit_deq(1)
        emit_w1(1)
        wt(sT, "t_m1b")
        if not dry:
            v.tensor_copy(mbf[:, 0:512], pq(0))
            v.tensor_copy(mbf[:, 512:1024], pq(1)).then_inc(sV.h, 1)
        rec("v_m1row", sV)
        for k in range(DC + 2):
            if k < DC:
                wt(sT, f"t_tr{k}_0")
                if k >= 2:
                    wt(sT, f"t_f{k-2}_1")
                if not dry:
                    v.tensor_copy(SwT(k)[:, 0:1024], p_t0[:]).then_inc(sV.h, 1)
                rec(f"v_tre{4*k}", sV)
            if 1 <= k <= DC:
                wt(sT, f"t_f{k-1}_0")
                if not dry:
                    v.tensor_add(Bch(k - 1, 0), Ach(k - 1, 0),
                                 pool4((2 * (k - 1)) % 4))
                    v.scalar_tensor_tensor(Bch(k - 1, 0), mbf[:, 0:512],
                                           w1n[:, k - 1:k], Bch(k - 1, 0),
                                           ALU.mult, ALU.add).then_inc(sV.h, 1)
                rec(f"v_V{k-1}_0", sV)
            if k < DC:
                wt(sT, f"t_tr{k}_2")
                if not dry:
                    v.tensor_copy(SwT(k)[:, 2048:3072], p_t0[:]).then_inc(sV.h, 1)
                rec(f"v_tre{4*k+2}", sV)
                if k + 2 < DC:
                    emit_deq(k + 2)
                    emit_w1(k + 2)
            if 1 <= k <= DC:
                wt(sT, f"t_f{k-1}_1")
                if not dry:
                    v.tensor_add(Bch(k - 1, 1), Ach(k - 1, 1),
                                 pool4((2 * (k - 1) + 1) % 4))
                    v.scalar_tensor_tensor(Bch(k - 1, 1), mbf[:, 512:1024],
                                           w1n[:, k - 1:k], Bch(k - 1, 1),
                                           ALU.mult, ALU.add).then_inc(sV.h, 1)
                rec(f"v_V{k-1}_1", sV)
        wt(sT, "t_st2_31_1")
        if not dry:
            v.tensor_copy(ext_dst, p_st[0:4, :]).then_inc(sV.h, 1)
        rec("v_st2evac", sV)
        wt(sT, "t_ext2")
        extb = (p_b1[0:1, :], p_st[0:1, :], p_q[0:1, 0:512], p_q[0:1, 512:1024])
        if not dry:
            for j in range(4):
                dst = (m_slot[:, 0:512], v_slot[:, 0:512],
                       m_slot[:, 512:1024], v_slot[:, 512:1024])[j]
                i = v.tensor_copy(dst, extb[j])
            i.then_inc(sV.h, 1)
        rec("v_ext2", sV)
        if not dry:
            v.scalar_tensor_tensor(TT2[0:1, 0:1024], rowsb[:], EPS_LN,
                                   rowsb[:], ALU.mult, ALU.mult)
            v.tensor_scalar_mul(m_slot, m_slot, 1.0 / 4096.0)
            v.tensor_scalar_mul(v_slot, v_slot, 1.0 / 4096.0)
            v.tensor_mul(rowsb[:].bitcast(F32)[:, 0:512], m_slot[:, 0:512],
                         m_slot[:, 0:512])
            v.tensor_sub(v_slot[:, 0:512], v_slot[:, 0:512],
                         rowsb[:].bitcast(F32)[:, 0:512])
            v.tensor_mul(rowsb[:].bitcast(F32)[:, 0:512], m_slot[:, 512:1024],
                         m_slot[:, 512:1024])
            v.tensor_sub(v_slot[:, 512:1024], v_slot[:, 512:1024],
                         rowsb[:].bitcast(F32)[:, 0:512])
            v.tensor_add(v_slot, v_slot, TT2[0:1, 0:1024]).then_inc(sV.h, 1)
        rec("v_rows2", sV)
        wt(sS, "s_r2")
        if not dry:
            v.tensor_mul(m_slot, m_slot, v_slot)
            v.tensor_scalar_mul(m_slot, m_slot, -1.0).then_inc(sV.h, 1)
        rec("v_c2", sV)
        wt(sT, "t_p4b")
        if not dry:
            v.tensor_copy(p4r[:, 0:512], pq(0))
            v.tensor_copy(p4r[:, 512:1024], pq(1))
            v.tensor_copy(p4r[:, 1024:1536], p_dn[:, 0:512])
            v.tensor_copy(p4r[:, 1536:2048], p_dn[:, 512:1024]).then_inc(sV.h, 1)
        rec("v_p4r", sV)
        for oc in range(DC):
            if oc >= 2:
                wt(s_st[oc % 2], 16 * (oc // 2))
            if not dry:
                v.tensor_mul(Oslot(oc), Bch(oc), p4r[:, 0:1024])
                i = v.tensor_add(Oslot(oc), Oslot(oc), p4r[:, 1024:2048])
            if not dry:
                i.then_inc(sV.h, 1)
            rec(f"v_out{oc}", sV)

    def prog_scalar(s, dry):
        wt = _mk_waiter(s, dry, P)
        rec = (P.rec if dry else P.chk)

        wt(s_c, 16 * 7)
        for h in range(NH):
            for half in range(2):
                bank = half
                wt(sV, f"v_t1{h}_{half}")
                if not dry:
                    s.activation(t2[:], t1[:], AF.Exp, bias=bqs[:, h:h + 1])
                    s.activation(t1[:], pq(bank), AF.Relu,
                                 bias=bqs[:, h:h + 1]).then_inc(sS.h, 1)
                rec(f"s_relu{h}_{half}", sS)
        for h in range(NH):
            for half in range(2):
                k2 = 2 * h + half
                wt(sT, f"t_num{h}_{half}")
                if not dry:
                    s.copy(Rretr(h, half),
                           p_dn[:, (k2 % 2) * 512:(k2 % 2) * 512 + 512]
                           ).then_inc(sS.h, 1)
                rec(f"s_retr{h}_{half}", sS)
        for oc in range(DC):
            for half in range(2):
                wt(sV, f"v_res{oc}_{half}")
                if oc > 0:
                    wt(sT, f"t_st1_{oc-1}_{half}")
                if not dry:
                    s.activation(usqh(half), Ach(oc, half),
                                 AF.Square).then_inc(sS.h, 1)
                rec(f"s_usq{oc}_{half}", sS)
        wt(sV, "v_rows1")
        lnscr = rowsb[:].bitcast(F32)[:, 0:512]
        if not dry:
            s.activation(lnscr, v_slot[:, 0:512], AF.Ln)
            s.activation(v_slot[:, 0:512], lnscr, AF.Exp, scale=0.5)
            s.activation(lnscr, v_slot[:, 512:1024], AF.Ln)
            s.activation(v_slot[:, 512:1024], lnscr, AF.Exp,
                         scale=0.5).then_inc(sS.h, 1)
        rec("s_s1d", sS)

        for k in range(DC + 2):
            if k < DC:
                wt(sT, f"t_tr{k}_1")
                if k >= 2:
                    wt(sT, f"t_f{k-2}_1")
                if not dry:
                    s.copy(SwT(k)[:, 1024:2048], p_t1[:]).then_inc(sS.h, 1)
                rec(f"s_tre{4*k+1}", sS)
            if 1 <= k <= DC:
                wt(sV, f"v_V{k-1}_0")
                if k - 1 > 0:
                    wt(sT, f"t_st2_{k-2}_0")
                if not dry:
                    s.activation(usqh(0), Bch(k - 1, 0),
                                 AF.Square).then_inc(sS.h, 1)
                rec(f"s_vsq{k-1}_0", sS)
            if k < DC:
                wt(sT, f"t_tr{k}_3")
                if not dry:
                    s.copy(SwT(k)[:, 3072:4096], p_t1[:]).then_inc(sS.h, 1)
                rec(f"s_tre{4*k+3}", sS)
            if 1 <= k <= DC:
                wt(sV, f"v_V{k-1}_1")
                if k - 1 > 0:
                    wt(sT, f"t_st2_{k-2}_1")
                if not dry:
                    s.activation(usqh(1), Bch(k - 1, 1),
                                 AF.Square).then_inc(sS.h, 1)
                rec(f"s_vsq{k-1}_1", sS)
        wt(sV, "v_rows2")
        if not dry:
            s.activation(lnscr, v_slot[:, 0:512], AF.Ln)
            s.activation(v_slot[:, 0:512], lnscr, AF.Exp, scale=-0.5)
            s.activation(lnscr, v_slot[:, 512:1024], AF.Ln)
            s.activation(v_slot[:, 512:1024], lnscr, AF.Exp,
                         scale=-0.5).then_inc(sS.h, 1)
        rec("s_r2", sS)

    for dry in (True, False):
        for sem in (sG, sT, sV, sS, s_m, s_fb, s_c, s_sz, *s_x,
                    *s_wq, *s_wo, *s_qw, *s_st):
            sem.n = 0
        if dry:
            prog_gpsimd(None, True)
            prog_sync(None, True)
            prog_tensor(None, True)
            prog_vector(None, True)
            prog_scalar(None, True)
        else:
            with nc.Block() as block:
                @block.gpsimd
                def _(g):
                    prog_gpsimd(g, False)

                @block.sync
                def _(sy):
                    prog_sync(sy, False)

                @block.tensor
                def _(t):
                    prog_tensor(t, False)

                @block.vector
                def _(v):
                    prog_vector(v, False)

                @block.scalar
                def _(s):
                    prog_scalar(s, False)
    return nc


def _prep_host(inputs):
    f32 = np.float32
    hs = np.asarray(inputs["L0_attn_hidden_states"], dtype=f32)[0]
    Wq = np.asarray(inputs["L0_attn_Wq"], dtype=f32)
    bq = np.asarray(inputs["L0_attn_bq"], dtype=f32)
    M = np.asarray(inputs["L0_attn_M"], dtype=f32)
    z = np.asarray(inputs["L0_attn_z"], dtype=f32)
    Wo = np.asarray(inputs["L0_attn_Wo"], dtype=f32)
    qw = np.asarray(inputs["L0_ffn_qweight"], dtype=np.int32)
    sc = np.asarray(inputs["L0_ffn_scales"], dtype=f32)
    zr = np.asarray(inputs["L0_ffn_scaled_zeros"], dtype=f32)
    fb = np.asarray(inputs["L0_ffn_bias"], dtype=f32)

    wq_img = np.ascontiguousarray(
        Wq.reshape(32, 128, 4096)[::4]
        .reshape(8, 128, 32, 128)
        .transpose(0, 3, 2, 1))
    wq_img = wq_img.reshape(8, 128, 4096)
    bq_img = np.ascontiguousarray(bq.reshape(32, 128)[::4].T)
    zmk = np.zeros((128, 64), dtype=f32)
    for h in range(8):
        zmk[:, 8 * h + h] = z[h]
    wo_img = np.ascontiguousarray(
        Wo.reshape(32, 128, 32, 128)
        .transpose(0, 3, 2, 1))
    wo_img = wo_img.reshape(32, 128, 4096)
    qw_img = np.ascontiguousarray(qw.reshape(4096, 4096).reshape(32, 128, 4096))
    sc_img = np.ascontiguousarray(
        sc.reshape(4096, 32).reshape(32, 128, 32).transpose(1, 0, 2)
        .reshape(128, 1024))
    zr_img = np.ascontiguousarray(
        zr.reshape(4096, 32).reshape(32, 128, 32).transpose(1, 0, 2)
        .reshape(128, 1024))
    fbr = np.ascontiguousarray(fb.reshape(1, 4096))
    epat = np.zeros((8, 1024), dtype=ml_dtypes.bfloat16)
    for h in range(8):
        epat[h, h * 128:(h + 1) * 128] = 1.0
    onesr = np.ones((1, 128), dtype=f32)
    slhs = np.zeros((128, 16), dtype=ml_dtypes.bfloat16)
    for k in range(4):
        slhs[:, 4 * k + k] = 1.0
    sel4 = np.eye(4, dtype=f32)
    idnt = np.eye(128, dtype=ml_dtypes.bfloat16)
    cvals = np.array([[EPS_LN, 0.5, -0.5, 0.0]], dtype=f32)

    shared = dict(wq=wq_img, bqi=bq_img, mi=np.ascontiguousarray(M), zmk=zmk,
                  wo=wo_img, qwi=qw_img, sci=sc_img, zri=zr_img, fbr=fbr,
                  epat=epat, onesr=onesr, slhs=slhs, sel4=sel4, idnt=idnt,
                  cvals=cvals)
    in_maps = []
    for c in range(NCORES):
        xs = hs[c * T:(c + 1) * T]
        xh = np.ascontiguousarray(xs.T).reshape(DC, 128, T)
        m = dict(shared)
        m["xh"] = xh
        in_maps.append(m)
    return in_maps


def kernel(**inputs):
    assert np.asarray(inputs["L0_attn_Wk"]).shape == (1024, 4096)
    assert np.asarray(inputs["L0_attn_Wv"]).shape == (1024, 4096)
    assert np.asarray(inputs["L0_attn_bk"]).shape == (1024,)
    assert np.asarray(inputs["L0_attn_bv"]).shape == (1024,)
    if "nc" not in _CACHE:
        _CACHE["nc"] = build_nc()
    nc = _CACHE["nc"]
    in_maps = _prep_host(inputs)
    res = run_bass_kernel_spmd(nc, in_maps, core_ids=list(range(NCORES)))
    out = np.empty((1, 8192, 4096), dtype=np.float32)
    for c in range(NCORES):
        oT = np.asarray(res.results[c]["outT"], dtype=np.float32)
        out[0, c * T:(c + 1) * T, :] = oT.reshape(4096, T).T
    return out


# version 17
# speedup vs baseline: 1.2340x; 1.2340x over previous
import numpy as np
import ml_dtypes
import concourse.bass as bass
import concourse.mybir as mybir
from concourse.bass_utils import run_bass_kernel_spmd

F32 = mybir.dt.float32
BF16 = mybir.dt.bfloat16
I32 = mybir.dt.int32
AF = mybir.ActivationFunctionType
ALU = mybir.AluOpType

NCORES = 8
T = 1024
TH = 512
DC = 32
NH = 8
EPS_ATTN = 1e-6
EPS_LN = 1e-5

_CACHE = {}


class Prog:

    def __init__(self):
        self.ev = {}
        self.dry = True

    def rec(self, name, sem, k=1):
        sem.n += k
        self.ev[name] = sem.n

    def chk(self, name, sem, k=1):
        sem.n += k
        assert self.ev[name] == sem.n, (name, self.ev[name], sem.n)


class Sem:
    def __init__(self, nc, name):
        self.h = nc.alloc_semaphore(name)
        self.name = name
        self.n = 0


def _mk_waiter(eng, dry, P):
    last = {}

    def wait(sem, val):
        if dry:
            return
        if isinstance(val, str):
            val = P.ev[val]
        if val <= 0:
            return
        if last.get(sem.name, 0) >= val:
            return
        last[sem.name] = val
        eng.wait_ge(sem.h, val)

    return wait


def build_nc(debug_dump=False):
    nc = bass.Bass("TRN2", target_bir_lowering=False,
                   detect_race_conditions=False)

    d_x = nc.dram_tensor("xh", [DC, 128, T], F32, kind="ExternalInput").ap()
    d_wq = nc.dram_tensor("wq", [NH, 128, 4096], F32, kind="ExternalInput").ap()
    d_bq = nc.dram_tensor("bqi", [128, NH], F32, kind="ExternalInput").ap()
    d_m = nc.dram_tensor("mi", [NH, 128, 128], F32, kind="ExternalInput").ap()
    d_zm = nc.dram_tensor("zmk", [128, 64], F32, kind="ExternalInput").ap()
    d_wo = nc.dram_tensor("wo", [DC, 128, 4096], F32, kind="ExternalInput").ap()
    d_qw = nc.dram_tensor("qwi", [DC, 128, 4096], I32, kind="ExternalInput").ap()
    d_sc = nc.dram_tensor("sci", [128, 1024], F32, kind="ExternalInput").ap()
    d_zr = nc.dram_tensor("zri", [128, 1024], F32, kind="ExternalInput").ap()
    d_fb = nc.dram_tensor("fbr", [1, 4096], F32, kind="ExternalInput").ap()
    d_ep = nc.dram_tensor("epat", [8, 1024], BF16, kind="ExternalInput").ap()
    d_on = nc.dram_tensor("onesr", [1, 128], F32, kind="ExternalInput").ap()
    d_sl = nc.dram_tensor("slhs", [128, 16], BF16, kind="ExternalInput").ap()
    d_s4 = nc.dram_tensor("sel4", [4, 4], F32, kind="ExternalInput").ap()
    d_id = nc.dram_tensor("idnt", [128, 128], BF16, kind="ExternalInput").ap()
    d_cv = nc.dram_tensor("cvals", [1, 4], F32, kind="ExternalInput").ap()
    d_out = nc.dram_tensor("outT", [DC, 128, T], F32, kind="ExternalOutput").ap()
    if debug_dump:
        d_dbgA = nc.dram_tensor("dbgA", [128, DC * T], BF16, kind="ExternalOutput").ap()
        d_dbgB = nc.dram_tensor("dbgB", [128, DC * T], BF16, kind="ExternalOutput").ap()
        d_dbgS = nc.dram_tensor("dbgS", [128, NH * T], BF16, kind="ExternalOutput").ap()
        d_dbgR = nc.dram_tensor("dbgR", [128, NH * T], BF16, kind="ExternalOutput").ap()
        d_dbgrows = nc.dram_tensor("dbgrows", [1, 2048], F32, kind="ExternalOutput").ap()
        d_dbgFf = nc.dram_tensor("dbgFf", [128, 2048], F32, kind="ExternalOutput").ap()

    A = nc.alloc_sbuf_tensor("A", [128, DC * T], BF16)
    B = nc.alloc_sbuf_tensor("B", [128, DC * T], BF16)
    S = nc.alloc_sbuf_tensor("S", [128, NH * T], BF16)
    R = nc.alloc_sbuf_tensor("R", [128, NH * T], BF16)
    W = nc.alloc_sbuf_tensor("W", [128, 8192], BF16)
    Ff = nc.alloc_sbuf_tensor("Ff", [128, 2048], F32)
    TT2 = nc.alloc_sbuf_tensor("TT2", [128, 1024], F32)
    usq = nc.alloc_sbuf_tensor("usq", [128, 2 * TH], BF16)
    mbf = nc.alloc_sbuf_tensor("mbf", [128, NH * 128], BF16)
    rows = nc.alloc_sbuf_tensor("rows", [1, 2048], F32)
    rowsb = nc.alloc_sbuf_tensor("rowsb", [1, 1024], BF16)
    bqs = nc.alloc_sbuf_tensor("bqs", [128, NH], F32)
    nbq = nc.alloc_sbuf_tensor("nbq", [128, NH], F32)
    zmks = nc.alloc_sbuf_tensor("zmks", [128, 64], BF16)
    epats = nc.alloc_sbuf_tensor("epats", [8, 1024], BF16)
    onesr = nc.alloc_sbuf_tensor("onesrs", [1, 128], F32)
    slhss = nc.alloc_sbuf_tensor("slhss", [128, 16], BF16)
    sel4s = nc.alloc_sbuf_tensor("sel4s", [4, 4], F32)
    idnts = nc.alloc_sbuf_tensor("idnts", [128, 128], BF16)
    cvs = nc.alloc_sbuf_tensor("cvs", [1, 4], F32)
    nc.const_aps.register(F32, EPS_LN, cvs[0:1, 0:1])
    nc.const_aps.register(F32, 0.5, cvs[0:1, 1:2])
    nc.const_aps.register(F32, -0.5, cvs[0:1, 2:3])

    p_q = nc.alloc_psum_tensor("p_q", [128, 1024], F32)
    p_dn = nc.alloc_psum_tensor("p_dn", [128, 1024], F32)
    p_b1 = nc.alloc_psum_tensor("p_b1", [128, 512], F32)
    p_t0 = nc.alloc_psum_tensor("p_t0", [128, 1024], BF16)
    p_t1 = nc.alloc_psum_tensor("p_t1", [128, 1024], BF16)
    p_st = nc.alloc_psum_tensor("p_st", [128, 512], F32)
    rem = nc.sbuf_bytes_remaining
    assert rem >= 0, f"SBUF over budget by {-rem}"

    sG = Sem(nc, "sG")
    sT = Sem(nc, "sT")
    sV = Sem(nc, "sV")
    sS = Sem(nc, "sS")
    s_x = [Sem(nc, f"s_x{k}") for k in range(4)]
    s_wq = [Sem(nc, f"s_wq{k}") for k in range(2)]
    s_m = Sem(nc, "s_m")
    s_fb = Sem(nc, "s_fb")
    s_wo = [Sem(nc, f"s_wo{k}") for k in range(2)]
    s_qw = [Sem(nc, f"s_qw{k}") for k in range(2)]
    s_c = Sem(nc, "s_c")
    s_sz = Sem(nc, "s_sz")
    s_st = [Sem(nc, f"s_st{k}") for k in range(2)]

    P = Prog()

    def Ach(c, half=None):
        if half is None:
            return A[:, c * T:(c + 1) * T]
        return A[:, c * T + half * TH: c * T + (half + 1) * TH]

    def Bch(c, half=None):
        if half is None:
            return B[:, c * T:(c + 1) * T]
        return B[:, c * T + half * TH: c * T + (half + 1) * TH]

    def Ssig(h, half=None):
        if half is None:
            return S[:, h * T:(h + 1) * T]
        return S[:, h * T + half * TH: h * T + (half + 1) * TH]

    def Rretr(h, half=None):
        if half is None:
            return R[:, h * T:(h + 1) * T]
        return R[:, h * T + half * TH: h * T + (half + 1) * TH]

    def Wslot(i):
        return W[:, (i % 2) * 4096:(i % 2) * 4096 + 4096]

    def SwT(oc):
        return S[:, (oc % 2) * 4096:(oc % 2) * 4096 + 4096]

    r_addr = nc.lookup_mloc(R).addr
    ost = nc.alloc_sbuf_tensor_at("ost", [128, 4096], F32, offset=r_addr)
    s_addr = nc.lookup_mloc(S).addr
    p4r = nc.alloc_sbuf_tensor_at("p4r", [128, 2048], F32, offset=s_addr)
    m2bf = S[:, 4096:6144]

    def Opair(po):
        return ost[:, (po % 2) * 2048:(po % 2) * 2048 + 2048]

    fb_row = R[0:1, 0:4096]
    den = Ff[0:8, 0:1024]
    rden = Ff[0:8, 1024:2048]
    rden_bfT = nc.alloc_sbuf_tensor("rden_bfT", [8, 1024], BF16)
    rden_bf = rden_bfT[:]
    sc_img = Ff[:, 0:1024]
    zr_img = Ff[:, 1024:2048]
    t1 = TT2[:, 0:512]
    t2 = TT2[:, 512:1024]
    m_slot = rows[0:1, 0:1024]
    v_slot = rows[0:1, 1024:2048]

    ext_dst = TT2[0:4, 512:1024]

    def usqh(half):
        return usq[:, half * 512:(half + 1) * 512]

    def bcb(k2):
        return p_b1[:] if (k2 % 2 == 0) else p_st[:]

    def pool4(j):
        tsr = p_q if j < 2 else p_dn
        return tsr[:, (j % 2) * 512:(j % 2) * 512 + 512]

    def pq(bank, half=None):
        if half is None:
            return p_q[:, bank * 512:(bank + 1) * 512]
        return p_q[:, bank * 512:(bank + 1) * 512]

    def prog_gpsimd(g, dry):
        wt = _mk_waiter(g, dry, P)
        rec = (P.rec if dry else P.chk)

        def dma(out, in_, sem, name):
            if not dry:
                g.dma_start(out=out, in_=in_).then_inc(sem.h, 16)
            rec(name, sem, 16)

        dma(zmks[:], d_zm, s_m, "g_zm")
        dma(Wslot(0), d_wq[0], s_wq[0], "g_wq0")
        for c in range(0, 8):
            dma(Ach(c), d_x[c], s_x[c // 8], f"g_x{c}")
        dma(Wslot(1), d_wq[1], s_wq[1], "g_wq1")
        for h in range(NH):
            dma(mbf[:, h * 128:(h + 1) * 128], d_m[h], s_m, f"g_m{h}")
        for c in range(8, DC):
            dma(Ach(c), d_x[c], s_x[c // 8], f"g_x{c}")
        for h in range(2, NH):
            wt(sT, f"t_q{h-2}_1")
            dma(Wslot(h), d_wq[h], s_wq[h % 2], f"g_wq{h}")
        for oc in range(DC):
            i = NH + oc
            if oc == 0:
                wt(sT, "t_q6_1")
            elif oc == 1:
                wt(sT, "t_q7_1")
            else:
                wt(sV, f"v_fold{oc-2}")
            dma(Wslot(i)[:, 0:2048], d_wo[oc][:, 0:2048], s_wo[oc % 2],
                f"g_wo{oc}a")
            dma(Wslot(i)[:, 2048:4096], d_wo[oc][:, 2048:4096], s_wo[oc % 2],
                f"g_wo{oc}b")
        wt(sV, "v_fold30")
        dma(Wslot(NH + DC + 0), d_qw[0], s_qw[0], "g_qw0")
        wt(sV, "v_fold31")
        dma(Wslot(NH + DC + 1), d_qw[1], s_qw[1], "g_qw1")
        wt(sT, "t_att31_1")
        dma(fb_row, d_fb, s_fb, "g_fb")
        for oc in range(2, DC):
            i = NH + DC + oc
            wt(sT, f"t_trd{oc-2}")
            dma(Wslot(i), d_qw[oc], s_qw[oc % 2], f"g_qw{oc}")


    def prog_sync(sy, dry):
        wt = _mk_waiter(sy, dry, P)
        rec = (P.rec if dry else P.chk)

        def dma(out, in_, sem, name):
            if not dry:
                sy.dma_start(out=out, in_=in_).then_inc(sem.h, 16)
            rec(name, sem, 16)

        dma(bqs[:], d_bq, s_c, "c_bq")
        dma(epats[:], d_ep, s_c, "c_ep")
        dma(onesr[:], d_on, s_c, "c_on")
        dma(slhss[:], d_sl, s_c, "c_sl")
        dma(sel4s[:], d_s4, s_c, "c_s4")
        dma(idnts[:], d_id, s_c, "c_id")
        dma(cvs[:], d_cv, s_c, "c_cv")
        wt(sV, "v_rden")
        dma(sc_img, d_sc, s_sz, "c_sc")
        wt(sT, "t_bc7_1")
        dma(zr_img, d_zr, s_sz, "c_zr")
        for po in range(DC // 2):
            wt(sV, f"v_outp{po}")
            op = Opair(po)
            dma(d_out[2 * po], op[:, 0:1024], s_st[0], f"st{2 * po}")
            dma(d_out[2 * po + 1], op[:, 1024:2048], s_st[1], f"st{2 * po + 1}")
        if debug_dump:
            wt(sV, "v_outp15")
            for nm, dd, ss in (("dA", A, d_dbgA), ("dB", B, d_dbgB),
                               ("dS", S, d_dbgS), ("dR", R, d_dbgR),
                               ("drw", rows, d_dbgrows), ("dFf", Ff, d_dbgFf)):
                dma(ss, dd[:], s_c, nm)
        if not dry:
            sy.wait_ge(s_st[0].h, (DC // 2) * 16)
            sy.wait_ge(s_st[1].h, (DC // 2) * 16)
            if debug_dump:
                sy.wait_ge(s_c.h, P.ev["dFf"])

    def prog_tensor(t, dry):
        wt = _mk_waiter(t, dry, P)
        rec = (P.rec if dry else P.chk)

        def mm(out, lhsT, rhs, start, stop, **kw):
            if not dry:
                return t.matmul(out, lhsT, rhs, start=start, stop=stop, **kw)
            return None

        def den(h):
            wt(s_m, 16 * 9)
            wt(sV, f"v_sig{h}_1")
            for half in range(2):
                i = mm(p_dn[0:8, half * 512:(half + 1) * 512],
                       zmks[:, h * 8:(h + 1) * 8], Ssig(h, half),
                       start=(h == 0), stop=(h == NH - 1), skip_group_check=True)
            if h == NH - 1:
                if not dry:
                    i.then_inc(sT.h, 1)
                rec("t_den", sT)

        for h in range(NH):
            wt(s_wq[h % 2], 16 * (h // 2 + 1))
            for half in range(2):
                bank = half
                if h > 0:
                    wt(sS, f"s_relu{h-1}_{half}")
                for c in range(DC):
                    if h == 0 and half == 0:
                        wt(s_x[c // 8], 16 * 8)
                    i = mm(pq(bank), Wslot(h)[:, c * 128:(c + 1) * 128],
                           Ach(c, half), start=(c == 0), stop=(c == DC - 1))
                if not dry:
                    i.then_inc(sT.h, 1)
                rec(f"t_q{h}_{half}", sT)
            if h >= 2:
                den(h - 2)
        den(NH - 2)
        den(NH - 1)
        for k2 in range(17):
            if k2 < 16:
                h, half = divmod(k2, 2)
                wt(sV, "v_rden")
                if k2 >= 2:
                    ph, phalf = divmod(k2 - 2, 2)
                    wt(sV, f"v_qsel{ph}_{phalf}")
                i = mm(bcb(k2), epats[:, h * 128:(h + 1) * 128],
                       rden_bf[:, half * 512:(half + 1) * 512], True, True)
                if not dry:
                    i.then_inc(sT.h, 1)
                rec(f"t_bc{h}_{half}", sT)
            if k2 >= 1:
                h, half = divmod(k2 - 1, 2)
                wt(sV, f"v_qsel{h}_{half}")
                wt(s_m, 16 * 9)
                if k2 - 1 >= 2:
                    ph, phalf = divmod(k2 - 3, 2)
                    wt(sS, f"s_retr{ph}_{phalf}")
                i = mm(p_dn[:, ((k2 - 1) % 2) * 512:((k2 - 1) % 2) * 512 + 512],
                       mbf[:, h * 128:(h + 1) * 128], Ssig(h, half), True, True)
                if not dry:
                    i.then_inc(sT.h, 1)
                rec(f"t_num{h}_{half}", sT)
        def st1(oc, half):
            wt(sS, f"s_usq{oc}_{half}")
            first = (oc == 0 and half == 0)
            if first:
                wt(sV, "v_qsel7_1")
            mm(p_st[0:4, :], slhss[:, (2 * half) * 4:(2 * half) * 4 + 4],
               Ach(oc, half), start=first, stop=False, skip_group_check=True)
            i = mm(p_st[0:4, :], slhss[:, (2 * half + 1) * 4:(2 * half + 1) * 4 + 4],
                   usqh(half), start=False, stop=(oc == DC - 1 and half == 1),
                   skip_group_check=True)
            if not dry:
                i.then_inc(sT.h, 1)
            rec(f"t_st1_{oc}_{half}", sT)

        wt(sS, "s_retr7_1")
        for oc in range(DC):
            wt(sV, f"v_fold{oc}")
            for half in range(2):
                bank = (2 * oc + half) % 4
                if oc > 1:
                    wt(sV, f"v_res{oc-2}_{half}")
                for h in range(NH):
                    wt(sS, f"s_retr{h}_{half}")
                    i = mm(pool4(bank), Bch(oc)[:, h * 128:(h + 1) * 128],
                           Rretr(h, half), start=(h == 0), stop=(h == NH - 1))
                if not dry:
                    i.then_inc(sT.h, 1)
                rec(f"t_att{oc}_{half}", sT)
            if oc > 0:
                st1(oc - 1, 0)
                st1(oc - 1, 1)
        st1(DC - 1, 0)
        st1(DC - 1, 1)
        wt(sV, "v_st1evac")
        extb = (p_b1[0:1, :], p_st[0:1, :], p_q[0:1, 0:512], p_q[0:1, 512:1024])
        for j in range(4):
            i = mm(extb[j], sel4s[:, j:j + 1], ext_dst, True, True)
        if not dry:
            i.then_inc(sT.h, 1)
        rec(f"t_ext1", sT)
        wt(sV, "v_rows1")
        wt(sV, "v_res31_0")
        wt(sV, "v_res31_1")
        for half in range(2):
            i = mm(pq(half), onesr[:], m_slot[:, half * 512:(half + 1) * 512],
                   True, True)
        if not dry:
            i.then_inc(sT.h, 1)
        rec("t_m1b", sT)
        def trgrp(oc, grp):
            k = oc * 4 + grp
            pt = p_t0 if (k % 2 == 0) else p_t1
            if k >= 2:
                pk = k - 2
                if pk % 2 == 0:
                    wt(sV, f"v_tre{pk}")
                else:
                    wt(sS, f"s_tre{pk}")
            for j in range(8):
                g = grp * 8 + j
                if not dry:
                    i = t.transpose(pt[:, j * 128:(j + 1) * 128],
                                    Wslot(NH + DC + oc)[:, g * 128:(g + 1) * 128],
                                    idnts[:])
            if not dry:
                i.then_inc(sT.h, 1)
            rec(f"t_tr{oc}_{grp}", sT)
            if grp == 3:
                P.ev[f"t_trd{oc}"] = P.ev[f"t_tr{oc}_3"]

        def fhalf(oc, half):
            wt(sV, "v_centdone")
            wt(sV, f"v_tre{oc*4+2}")
            wt(sS, f"s_tre{oc*4+3}")
            wt(s_fb, 16)
            wt(sV, "v_s1dbf")
            bank = (2 * oc + half) % 4
            if oc > 1:
                wt(sV, f"v_V{oc-2}_{half}")
            for g in range(DC):
                mm(pool4(bank), SwT(oc)[:, g * 128:(g + 1) * 128],
                   Ach(g, half), start=(g == 0), stop=False)
            i = mm(pool4(bank), fb_row[:, oc * 128:(oc + 1) * 128],
                   rowsb[:, half * 512:(half + 1) * 512],
                   start=False, stop=True)
            if not dry:
                i.then_inc(sT.h, 1)
            rec(f"t_f{oc}_{half}", sT)

        def st2(oc, half):
            if oc == 0 and half == 0:
                wt(sV, "v_st1evac")
            wt(sS, f"s_vsq{oc}_{half}")
            first = (oc == 0 and half == 0)
            mm(p_st[0:4, :], slhss[:, (2 * half) * 4:(2 * half) * 4 + 4],
               Bch(oc, half), start=first, stop=False, skip_group_check=True)
            i = mm(p_st[0:4, :], slhss[:, (2 * half + 1) * 4:(2 * half + 1) * 4 + 4],
                   usqh(half), start=False, stop=(oc == DC - 1 and half == 1),
                   skip_group_check=True)
            if not dry:
                i.then_inc(sT.h, 1)
            rec(f"t_st2_{oc}_{half}", sT)

        for k in range(DC + 2):
            if k < DC:
                wt(sV, f"v_deq{k}")
                wt(sS, f"s_deq{k}")
                trgrp(k, 0)
                trgrp(k, 1)
            if 1 <= k <= DC:
                fhalf(k - 1, 0)
            if k < DC:
                trgrp(k, 2)
                trgrp(k, 3)
            if 1 <= k <= DC:
                fhalf(k - 1, 1)
            if 2 <= k:
                st2(k - 2, 0)
                st2(k - 2, 1)
        wt(sV, "v_st2evac")
        extb = (p_b1[0:1, :], p_st[0:1, :], p_q[0:1, 0:512], p_q[0:1, 512:1024])
        for j in range(4):
            i = mm(extb[j], sel4s[:, j:j + 1], ext_dst, True, True)
        if not dry:
            i.then_inc(sT.h, 1)
        rec(f"t_ext2", sT)
        wt(sV, "v_mrows")
        wt(sV, "v_V31_0")
        wt(sV, "v_V31_1")
        for half in range(2):
            i = mm(p_dn[:, half * 512:(half + 1) * 512], onesr[:],
                   m_slot[:, half * 512:(half + 1) * 512], True, True)
        if not dry:
            i.then_inc(sT.h, 1)
        rec("t_p4m", sT)
        wt(sS, "s_r2")
        for half in range(2):
            i = mm(pq(half), onesr[:], v_slot[:, half * 512:(half + 1) * 512],
                   True, True)
        if not dry:
            i.then_inc(sT.h, 1)
        rec("t_p4r", sT)

    def prog_vector(v, dry):
        wt = _mk_waiter(v, dry, P)
        rec = (P.rec if dry else P.chk)

        wt(s_c, 16 * 7)
        if not dry:
            v.tensor_scalar_mul(nbq[:], bqs[:], -1.0)
        for h in range(NH):
            for half in range(2):
                bank = half
                wt(sT, f"t_q{h}_{half}")
                if not dry:
                    v.tensor_single_scalar(t1[:], pq(bank), nbq[:, h:h + 1],
                                           ALU.min).then_inc(sV.h, 1)
                rec(f"v_t1{h}_{half}", sV)
                wt(sS, f"s_relu{h}_{half}")
                if not dry:
                    v.tensor_add(Ssig(h, half), t1[:], t2[:]).then_inc(sV.h, 1)
                rec(f"v_sig{h}_{half}", sV)
        wt(sT, "t_den")
        if not dry:
            v.tensor_single_scalar(den[:], p_dn[0:8, :], EPS_ATTN, ALU.add)
            v.reciprocal(rden[:], den[:])
            v.tensor_copy(rden_bf, rden[:]).then_inc(sV.h, 1)
        rec("v_rden", sV)
        for h in range(NH):
            for half in range(2):
                k2 = 2 * h + half
                wt(sT, f"t_bc{h}_{half}")
                if not dry:
                    v.tensor_mul(Ssig(h, half), Ssig(h, half),
                                 bcb(k2)).then_inc(sV.h, 1)
                rec(f"v_qsel{h}_{half}", sV)
        def emit_fold(oc):
            wt(s_wo[oc % 2], 32 * (oc // 2 + 1))
            ws = Wslot(NH + oc)[:].rearrange("p (h g o) -> p h g o", h=8, g=4)
            bo = Bch(oc).rearrange("p (h o) -> p h o", h=8)
            if not dry:
                v.tensor_add(bo, ws[:, :, 0, :], ws[:, :, 1, :])
                v.tensor_add(bo, bo, ws[:, :, 2, :])
                v.tensor_add(bo, bo, ws[:, :, 3, :]).then_inc(sV.h, 1)
            rec(f"v_fold{oc}", sV)

        emit_fold(0)
        emit_fold(1)
        for oc in range(DC):
            for half in range(2):
                bank = (2 * oc + half) % 4
                wt(sT, f"t_att{oc}_{half}")
                if not dry:
                    v.tensor_add(Ach(oc, half), Ach(oc, half),
                                 pool4(bank)).then_inc(sV.h, 1)
                rec(f"v_res{oc}_{half}", sV)
            if oc + 2 < DC:
                emit_fold(oc + 2)
        wt(sT, "t_st1_31_1")
        if not dry:
            v.tensor_copy(ext_dst, p_st[0:4, :]).then_inc(sV.h, 1)
        rec("v_st1evac", sV)
        wt(sT, "t_ext1")
        extb = (p_b1[0:1, :], p_st[0:1, :], p_q[0:1, 0:512], p_q[0:1, 512:1024])
        if not dry:
            for j in range(4):
                dst = (m_slot[:, 0:512], v_slot[:, 0:512],
                       m_slot[:, 512:1024], v_slot[:, 512:1024])[j]
                i = v.tensor_copy(dst, extb[j])
            i.then_inc(sV.h, 1)
        rec(f"v_ext1", sV)
        if not dry:
            v.tensor_scalar_mul(m_slot, m_slot, 1.0 / 4096.0)
            v.tensor_scalar_mul(v_slot, v_slot, 1.0 / 4096.0)
            v.tensor_mul(TT2[0:1, 0:1024], m_slot, m_slot)
            v.tensor_sub(v_slot, v_slot, TT2[0:1, 0:1024])
            v.tensor_single_scalar(v_slot, v_slot, EPS_LN,
                                   ALU.add).then_inc(sV.h, 1)
        rec("v_rows1", sV)
        wt(sS, "s_s1d")
        if not dry:
            v.tensor_copy(rowsb[:], v_slot).then_inc(sV.h, 1)
        rec("v_s1dbf", sV)
        def emit_deq(oc):
            wt(s_qw[oc % 2], 16 * (oc // 2 + 1))
            wt(s_sz, 32)
            if oc >= 2:
                wt(sT, f"t_trd{oc-2}")
            ws = Wslot(NH + DC + oc)
            for g in range(16):
                col = oc * 32 + g
                if not dry:
                    ts = v.tensor_scalar(ws[:, g * 128:(g + 1) * 128],
                                         ws[:, g * 128:(g + 1) * 128],
                                         sc_img[:, col:col + 1],
                                         zr_img[:, col:col + 1],
                                         ALU.mult, ALU.add)
            if not dry:
                ts.then_inc(sV.h, 1)
            rec(f"v_deq{oc}", sV)

        emit_deq(0)
        emit_deq(1)
        wt(sT, "t_m1b")
        if not dry:
            v.tensor_copy(usq[:, 0:512], pq(0))
            v.tensor_copy(usq[:, 512:1024], pq(1)).then_inc(sV.h, 1)
        rec("v_m1row", sV)
        for oc in range(DC):
            if not dry:
                v.tensor_sub(Ach(oc), Ach(oc), usq[:]).then_inc(sV.h, 1)
            rec(f"v_cent{oc}", sV)
        P.ev["v_centdone"] = P.ev["v_cent31"]
        for k in range(DC + 2):
            if k < DC:
                wt(sT, f"t_tr{k}_0")
                if k >= 2:
                    wt(sT, f"t_f{k-2}_1")
                if not dry:
                    v.tensor_copy(SwT(k)[:, 0:1024], p_t0[:]).then_inc(sV.h, 1)
                rec(f"v_tre{4*k}", sV)
            if 1 <= k <= DC:
                wt(sT, f"t_f{k-1}_0")
                if not dry:
                    v.tensor_add(Bch(k - 1, 0), Ach(k - 1, 0),
                                 pool4((2 * (k - 1)) % 4)).then_inc(sV.h, 1)
                rec(f"v_V{k-1}_0", sV)
            if k < DC:
                wt(sT, f"t_tr{k}_2")
                if not dry:
                    v.tensor_copy(SwT(k)[:, 2048:3072], p_t0[:]).then_inc(sV.h, 1)
                rec(f"v_tre{4*k+2}", sV)
                if k + 2 < DC:
                    emit_deq(k + 2)
            if 1 <= k <= DC:
                wt(sT, f"t_f{k-1}_1")
                if not dry:
                    v.tensor_add(Bch(k - 1, 1), Ach(k - 1, 1),
                                 pool4((2 * (k - 1) + 1) % 4)).then_inc(sV.h, 1)
                rec(f"v_V{k-1}_1", sV)
        wt(sT, "t_st2_31_1")
        if not dry:
            v.tensor_copy(ext_dst, p_st[0:4, :]).then_inc(sV.h, 1)
        rec("v_st2evac", sV)
        wt(sT, "t_ext2")
        extb = (p_b1[0:1, :], p_st[0:1, :], p_q[0:1, 0:512], p_q[0:1, 512:1024])
        if not dry:
            v.scalar_tensor_tensor(TT2[0:1, 0:1024], rowsb[:], EPS_LN,
                                   rowsb[:], ALU.mult, ALU.mult)
            v.tensor_scalar_mul(m_slot[:, 0:512], extb[0], -1.0 / 4096.0)
            v.tensor_scalar_mul(m_slot[:, 512:1024], extb[2],
                                -1.0 / 4096.0).then_inc(sV.h, 1)
        rec("v_mrows", sV)
        if not dry:
            v.tensor_scalar_mul(v_slot[:, 0:512], extb[1], 1.0 / 4096.0)
            v.tensor_scalar_mul(v_slot[:, 512:1024], extb[3], 1.0 / 4096.0)
            v.tensor_mul(rowsb[:].bitcast(F32)[:, 0:512], m_slot[:, 0:512],
                         m_slot[:, 0:512])
            v.tensor_sub(v_slot[:, 0:512], v_slot[:, 0:512],
                         rowsb[:].bitcast(F32)[:, 0:512])
            v.tensor_mul(rowsb[:].bitcast(F32)[:, 0:512], m_slot[:, 512:1024],
                         m_slot[:, 512:1024])
            v.tensor_sub(v_slot[:, 512:1024], v_slot[:, 512:1024],
                         rowsb[:].bitcast(F32)[:, 0:512])
            v.tensor_add(v_slot, v_slot, TT2[0:1, 0:1024]).then_inc(sV.h, 1)
        rec("v_rows2", sV)
        wt(sT, "t_p4m")
        if not dry:
            v.tensor_copy(m2bf[:, 0:512], p_dn[:, 0:512])
            v.tensor_copy(m2bf[:, 512:1024], p_dn[:, 512:1024])
            v.tensor_copy(m2bf[:, 1024:1536], p_dn[:, 0:512])
            v.tensor_copy(m2bf[:, 1536:2048],
                          p_dn[:, 512:1024]).then_inc(sV.h, 1)
        rec("v_m2bf", sV)
        wt(sT, "t_p4r")
        if not dry:
            v.tensor_copy(p4r[:, 0:512], pq(0))
            v.tensor_copy(p4r[:, 512:1024], pq(1))
            v.tensor_copy(p4r[:, 1024:1536], pq(0))
            v.tensor_copy(p4r[:, 1536:2048], pq(1)).then_inc(sV.h, 1)
        rec("v_p4r", sV)
        for po in range(DC // 2):
            if po >= 2:
                wt(s_st[0], 16 * (po - 1))
                wt(s_st[1], 16 * (po - 1))
            bp = B[:, po * 2048:(po + 1) * 2048]
            if not dry:
                v.tensor_add(bp, bp, m2bf)
                i = v.tensor_mul(Opair(po), bp, p4r[:, 0:2048])
                i.then_inc(sV.h, 1)
            rec(f"v_outp{po}", sV)

    def prog_scalar(s, dry):
        wt = _mk_waiter(s, dry, P)
        rec = (P.rec if dry else P.chk)

        wt(s_c, 16 * 7)
        for h in range(NH):
            for half in range(2):
                bank = half
                wt(sV, f"v_t1{h}_{half}")
                if not dry:
                    s.activation(t2[:], t1[:], AF.Exp, bias=bqs[:, h:h + 1])
                    s.activation(t1[:], pq(bank), AF.Relu,
                                 bias=bqs[:, h:h + 1]).then_inc(sS.h, 1)
                rec(f"s_relu{h}_{half}", sS)
        for h in range(NH):
            for half in range(2):
                k2 = 2 * h + half
                wt(sT, f"t_num{h}_{half}")
                if not dry:
                    s.copy(Rretr(h, half),
                           p_dn[:, (k2 % 2) * 512:(k2 % 2) * 512 + 512]
                           ).then_inc(sS.h, 1)
                rec(f"s_retr{h}_{half}", sS)
        for oc in range(DC):
            for half in range(2):
                wt(sV, f"v_res{oc}_{half}")
                if oc > 0:
                    wt(sT, f"t_st1_{oc-1}_{half}")
                if not dry:
                    s.activation(usqh(half), Ach(oc, half),
                                 AF.Square).then_inc(sS.h, 1)
                rec(f"s_usq{oc}_{half}", sS)
        wt(sV, "v_rows1")
        lnscr = rowsb[:].bitcast(F32)[:, 0:512]
        if not dry:
            s.activation(lnscr, v_slot[:, 0:512], AF.Ln)
            s.activation(v_slot[:, 0:512], lnscr, AF.Exp, scale=0.5)
            s.activation(lnscr, v_slot[:, 512:1024], AF.Ln)
            s.activation(v_slot[:, 512:1024], lnscr, AF.Exp,
                         scale=0.5).then_inc(sS.h, 1)
        rec("s_s1d", sS)

        def emit_deqS(oc):
            wt(s_qw[oc % 2], 16 * (oc // 2 + 1))
            wt(s_sz, 32)
            if oc >= 2:
                wt(sT, f"t_trd{oc-2}")
            ws = Wslot(NH + DC + oc)
            for g in range(16, 32):
                col = oc * 32 + g
                if not dry:
                    a = s.activation(ws[:, g * 128:(g + 1) * 128],
                                     ws[:, g * 128:(g + 1) * 128], AF.Identity,
                                     bias=zr_img[:, col:col + 1],
                                     scale=sc_img[:, col:col + 1])
            if not dry:
                a.then_inc(sS.h, 1)
            rec(f"s_deq{oc}", sS)

        emit_deqS(0)
        emit_deqS(1)
        for k in range(DC + 2):
            if k < DC:
                wt(sT, f"t_tr{k}_1")
                if k >= 2:
                    wt(sT, f"t_f{k-2}_1")
                if not dry:
                    s.copy(SwT(k)[:, 1024:2048], p_t1[:]).then_inc(sS.h, 1)
                rec(f"s_tre{4*k+1}", sS)
            if 1 <= k <= DC:
                wt(sV, f"v_V{k-1}_0")
                if k - 1 > 0:
                    wt(sT, f"t_st2_{k-2}_0")
                if not dry:
                    s.activation(usqh(0), Bch(k - 1, 0),
                                 AF.Square).then_inc(sS.h, 1)
                rec(f"s_vsq{k-1}_0", sS)
            if k < DC:
                wt(sT, f"t_tr{k}_3")
                if not dry:
                    s.copy(SwT(k)[:, 3072:4096], p_t1[:]).then_inc(sS.h, 1)
                rec(f"s_tre{4*k+3}", sS)
                if k + 2 < DC:
                    emit_deqS(k + 2)
            if 1 <= k <= DC:
                wt(sV, f"v_V{k-1}_1")
                if k - 1 > 0:
                    wt(sT, f"t_st2_{k-2}_1")
                if not dry:
                    s.activation(usqh(1), Bch(k - 1, 1),
                                 AF.Square).then_inc(sS.h, 1)
                rec(f"s_vsq{k-1}_1", sS)
        wt(sV, "v_rows2")
        if not dry:
            s.activation(lnscr, v_slot[:, 0:512], AF.Ln)
            s.activation(v_slot[:, 0:512], lnscr, AF.Exp, scale=-0.5)
            s.activation(lnscr, v_slot[:, 512:1024], AF.Ln)
            s.activation(v_slot[:, 512:1024], lnscr, AF.Exp,
                         scale=-0.5).then_inc(sS.h, 1)
        rec("s_r2", sS)

    for dry in (True, False):
        for sem in (sG, sT, sV, sS, s_m, s_fb, s_c, s_sz, *s_x,
                    *s_wq, *s_wo, *s_qw, *s_st):
            sem.n = 0
        if dry:
            prog_gpsimd(None, True)
            prog_sync(None, True)
            prog_tensor(None, True)
            prog_vector(None, True)
            prog_scalar(None, True)
        else:
            with nc.Block() as block:
                @block.gpsimd
                def _(g):
                    prog_gpsimd(g, False)

                @block.sync
                def _(sy):
                    prog_sync(sy, False)

                @block.tensor
                def _(t):
                    prog_tensor(t, False)

                @block.vector
                def _(v):
                    prog_vector(v, False)

                @block.scalar
                def _(s):
                    prog_scalar(s, False)
    return nc


def _prep_host(inputs):
    f32 = np.float32
    hs = np.asarray(inputs["L0_attn_hidden_states"], dtype=f32)[0]
    Wq = np.asarray(inputs["L0_attn_Wq"], dtype=f32)
    bq = np.asarray(inputs["L0_attn_bq"], dtype=f32)
    M = np.asarray(inputs["L0_attn_M"], dtype=f32)
    z = np.asarray(inputs["L0_attn_z"], dtype=f32)
    Wo = np.asarray(inputs["L0_attn_Wo"], dtype=f32)
    qw = np.asarray(inputs["L0_ffn_qweight"], dtype=np.int32)
    sc = np.asarray(inputs["L0_ffn_scales"], dtype=f32)
    zr = np.asarray(inputs["L0_ffn_scaled_zeros"], dtype=f32)
    fb = np.asarray(inputs["L0_ffn_bias"], dtype=f32)

    wq_img = np.ascontiguousarray(
        Wq.reshape(32, 128, 4096)[::4]
        .reshape(8, 128, 32, 128)
        .transpose(0, 3, 2, 1))
    wq_img = wq_img.reshape(8, 128, 4096)
    bq_img = np.ascontiguousarray(bq.reshape(32, 128)[::4].T)
    zmk = np.zeros((128, 64), dtype=f32)
    for h in range(8):
        zmk[:, 8 * h + h] = z[h]
    wo_img = np.ascontiguousarray(
        Wo.reshape(32, 128, 32, 128)
        .transpose(0, 3, 2, 1))
    wo_img = wo_img.reshape(32, 128, 4096)
    qw_img = np.ascontiguousarray(qw.reshape(4096, 4096).reshape(32, 128, 4096))
    sc_img = np.ascontiguousarray(
        sc.reshape(4096, 32).reshape(32, 128, 32).transpose(1, 0, 2)
        .reshape(128, 1024))
    zr_img = np.ascontiguousarray(
        zr.reshape(4096, 32).reshape(32, 128, 32).transpose(1, 0, 2)
        .reshape(128, 1024))
    fbr = np.ascontiguousarray(fb.reshape(1, 4096))
    epat = np.zeros((8, 1024), dtype=ml_dtypes.bfloat16)
    for h in range(8):
        epat[h, h * 128:(h + 1) * 128] = 1.0
    onesr = np.ones((1, 128), dtype=f32)
    slhs = np.zeros((128, 16), dtype=ml_dtypes.bfloat16)
    for k in range(4):
        slhs[:, 4 * k + k] = 1.0
    sel4 = np.eye(4, dtype=f32)
    idnt = np.eye(128, dtype=ml_dtypes.bfloat16)
    cvals = np.array([[EPS_LN, 0.5, -0.5, 0.0]], dtype=f32)

    shared = dict(wq=wq_img, bqi=bq_img, mi=np.ascontiguousarray(M), zmk=zmk,
                  wo=wo_img, qwi=qw_img, sci=sc_img, zri=zr_img, fbr=fbr,
                  epat=epat, onesr=onesr, slhs=slhs, sel4=sel4, idnt=idnt,
                  cvals=cvals)
    in_maps = []
    for c in range(NCORES):
        xs = hs[c * T:(c + 1) * T]
        xh = np.ascontiguousarray(xs.T).reshape(DC, 128, T)
        m = dict(shared)
        m["xh"] = xh
        in_maps.append(m)
    return in_maps


def kernel(**inputs):
    assert np.asarray(inputs["L0_attn_Wk"]).shape == (1024, 4096)
    assert np.asarray(inputs["L0_attn_Wv"]).shape == (1024, 4096)
    assert np.asarray(inputs["L0_attn_bk"]).shape == (1024,)
    assert np.asarray(inputs["L0_attn_bv"]).shape == (1024,)
    if "nc" not in _CACHE:
        _CACHE["nc"] = build_nc()
    nc = _CACHE["nc"]
    in_maps = _prep_host(inputs)
    res = run_bass_kernel_spmd(nc, in_maps, core_ids=list(range(NCORES)))
    out = np.empty((1, 8192, 4096), dtype=np.float32)
    for c in range(NCORES):
        oT = np.asarray(res.results[c]["outT"], dtype=np.float32)
        out[0, c * T:(c + 1) * T, :] = oT.reshape(4096, T).T
    return out
